# version 02 (final)
# baseline (speedup 1.0000x reference)
import contextlib
import numpy as np

import concourse.bass as bass
import concourse.mybir as mybir
from concourse.bass_utils import run_bass_kernel_spmd

F32 = mybir.dt.float32
ALU = mybir.AluOpType
AF = mybir.ActivationFunctionType
AXX = mybir.AxisListType.X

D = 1024
E = 2048
H = 3072
NST = 16
RNK = 64
B_SZ = 2
L_SEQ = 16384
NCORE = 8
TCORE = L_SEQ // 4
HALO = 3
TT = TCORE + HALO
MAGIC = 12582912.0
QA = 32767.0
WQ = 7.0
BLK = 32
EPS = 1e-5
TCH = 512
SCH = 512
NSCH = TCORE // SCH

OF_A = 0
OF_DTB = 256
OF_CW = 272
OF_CB = 336
OF_D = 352
OF_U = 368
OF_V = 371
OF_ONE = 374
OF_ONEROW = 376

SC_XI, SC_X, SC_UI, SC_U, SC_GI, SC_G, SC_GSI, SC_GS, SC_OI, SC_OH = range(10)


def tt_chunks():
    return [(0, TCH)] + [(HALO + i * TCH, TCH) for i in range(TCORE // TCH)]


def t_chunks(width=TCH, total=TCORE, off=0):
    return [(off + i * width, min(width, total - i * width))
            for i in range((total + width - 1) // width)]


class Emit:
    def __init__(self):
        nc = bass.Bass("TRN2", target_bir_lowering=False, debug=False)
        self.nc = nc
        self.es = contextlib.ExitStack()
        g = {}
        g["xT"] = nc.dram_tensor("xT", [D, TT], F32, kind="ExternalInput").ap()
        g["w_inT"] = nc.dram_tensor("w_inT", [D, 2 * E], F32, kind="ExternalInput").ap()
        g["w_outT"] = nc.dram_tensor("w_outT", [E, D], F32, kind="ExternalInput").ap()
        g["w_xpT"] = nc.dram_tensor("w_xpT", [E, 128], F32, kind="ExternalInput").ap()
        g["w_dtT"] = nc.dram_tensor("w_dtT", [RNK, E], F32, kind="ExternalInput").ap()
        g["w_up"] = nc.dram_tensor("w_up", [H, D], F32, kind="ExternalInput").ap()
        g["w_gate"] = nc.dram_tensor("w_gate", [H, D], F32, kind="ExternalInput").ap()
        g["w_down"] = nc.dram_tensor("w_down", [D, H], F32, kind="ExternalInput").ap()
        g["smalls"] = nc.dram_tensor("smalls", [128, 512], F32, kind="ExternalInput").ap()
        g["ident"] = nc.dram_tensor("ident", [128, 128], F32, kind="ExternalInput").ap()
        g["outT"] = nc.dram_tensor("outT", [D, TCORE], F32, kind="ExternalOutput").ap()
        g["s_wupT"] = nc.dram_tensor("s_wupT", [D, H], F32, kind="Internal").ap()
        g["s_wgateT"] = nc.dram_tensor("s_wgateT", [D, H], F32, kind="Internal").ap()
        g["s_wdownT"] = nc.dram_tensor("s_wdownT", [H, D], F32, kind="Internal").ap()
        g["s_xln"] = nc.dram_tensor("s_xln", [D, TT], F32, kind="Internal").ap()
        g["s_up"] = nc.dram_tensor("s_up", [H, TT], F32, kind="Internal").ap()
        g["s_gate"] = nc.dram_tensor("s_gate", [H, TT], F32, kind="Internal").ap()
        g["s_gs"] = nc.dram_tensor("s_gs", [H, TT], F32, kind="Internal").ap()
        g["s_xres1"] = nc.dram_tensor("s_xres1", [D, TT], F32, kind="Internal").ap()
        g["s_xres2"] = nc.dram_tensor("s_xres2", [D, TCORE], F32, kind="Internal").ap()
        g["s_xc"] = nc.dram_tensor("s_xc", [E, TT], F32, kind="Internal").ap()
        g["s_sz"] = nc.dram_tensor("s_sz", [E, TT], F32, kind="Internal").ap()
        g["s_xa"] = nc.dram_tensor("s_xa", [E, TCORE], F32, kind="Internal").ap()
        g["s_delta"] = nc.dram_tensor("s_delta", [E, TCORE], F32, kind="Internal").ap()
        g["s_yg"] = nc.dram_tensor("s_yg", [E, TCORE], F32, kind="Internal").ap()
        g["s_bt"] = nc.dram_tensor("s_bt", [NSCH, NST, SCH], F32, kind="Internal").ap()
        g["s_ct"] = nc.dram_tensor("s_ct", [NSCH, NST, SCH], F32, kind="Internal").ap()
        g["cc_in"] = nc.dram_tensor("cc_in", [1, 64], F32, kind="Internal")
        g["cc_out"] = nc.dram_tensor("cc_out", [1, 64], F32, kind="Internal")
        g["ag_in"] = nc.dram_tensor("ag_in", [128, 512], F32, kind="Internal")
        g["ag_out"] = nc.dram_tensor("ag_out", [512, 512], F32, kind="Internal")
        g["s_tmp1"] = nc.dram_tensor("s_tmp1", [1, 64], F32, kind="Internal").ap()
        self.g = g

        self.pool = nc.alloc_sbuf_tensor("pool", [128, 40960], F32)
        self.sm = nc.alloc_sbuf_tensor("sm", [128, 512], F32)
        self.ident = nc.alloc_sbuf_tensor("identsb", [128, 128], F32)
        self.a_scan = nc.alloc_sbuf_tensor("a_scan", [128, 256], F32)
        self.carry = nc.alloc_sbuf_tensor("carry", [128, 256], F32)
        self.p_sb = nc.alloc_sbuf_tensor("p_sb", [128, 256], F32)
        self.h_in = nc.alloc_sbuf_tensor("h_in", [128, 256], F32)
        self.agl = nc.alloc_sbuf_tensor("agl", [128, 512], F32)
        self.sumdelta = nc.alloc_sbuf_tensor("sumdelta", [128, 16], F32)
        self.scales = nc.alloc_sbuf_tensor("scalesb", [128, 24], F32)
        self.rmax = nc.alloc_sbuf_tensor("rmaxsb", [128, 4], F32)
        self.smax = nc.alloc_sbuf_tensor("smaxsb", [1, 128], F32)
        self.sc1 = nc.alloc_sbuf_tensor("sc1sb", [1, 64], F32)
        self.ps = [self.es.enter_context(nc.psum_tensor("psb%d" % i, [128, 512], F32))
                   for i in range(6)]
        self.ps_st = [self.es.enter_context(nc.psum_tensor("psst%d" % i, [128, 512], F32))
                      for i in range(2)]

        self.sem = {}
        self.cnt = {}
        for name in ("qL", "qS", "qG", "cV", "cS", "cT", "cG", "cc", "bar"):
            self.sem[name] = self.es.enter_context(nc.semaphore(name))
            self.cnt[name] = 0
        self.block = self.es.enter_context(nc.Block())

    def dbg(self, name, src_ap, shape):
        if not DEBUG:
            return
        o = self.nc.dram_tensor("dbg_" + name, shape, F32, kind="ExternalOutput").ap()

        def _g(q):
            self.w(q, "qS", "cV", "cS")
            self.inc("qG", q.dma_start(out=o[:, :], in_=src_ap), 16)
            self.w(q, "qG")
        self.on("gpsimd", _g)

    def on(self, eng, fn):
        fn(getattr(self.nc, eng))

    def inc(self, name, instr, k=1):
        instr.then_inc(self.sem[name], k)
        self.cnt[name] += k
        return self.cnt[name]

    def w(self, eng, *names):
        for n in names:
            if self.cnt[n] > 0:
                eng.wait_ge(self.sem[n], self.cnt[n])

    def pe_guard(self, *wait_sems):
        def _tg(t):
            for n in wait_sems:
                if self.cnt[n] > 0:
                    t.wait_ge(self.sem[n], self.cnt[n])
            self.inc("cT", t.matmul(self.ps[0][:, 0:16], self.ident[:],
                                    self.ident[:, 0:16], start=True, stop=True))
        self.on("tensor", _tg)

        def _sg(s):
            self.w(s, "cT")
            self.inc("cS", s.copy(out=self.pool[:, 40944:40960], in_=self.ps[0][:, 0:16]))
        self.on("scalar", _sg)

        def _tw(t):
            self.w(t, "cS")
        self.on("tensor", _tw)

    def fence(self, eng, name, instr):
        self.inc(name, instr)
        eng.wait_ge(self.sem[name], self.cnt[name])

    def barrier(self):
        tgt = self.cnt["bar"] + 5
        dma_tot = {n: self.cnt[n] for n in ("qL", "qS", "qG")}
        for e in ("sync", "vector", "scalar", "tensor", "gpsimd"):
            def _f(eng):
                for n, v in dma_tot.items():
                    if v:
                        eng.wait_ge(self.sem[n], v)
                eng.sem_inc(self.sem["bar"], 1)
                eng.wait_ge(self.sem["bar"], tgt)
            self.on(e, _f)
        self.cnt["bar"] = tgt

    def smc(self, col, n=1):
        return self.sm[:, col:col + n]

    def onerow(self):
        return self.sm[0:1, OF_ONEROW:OF_ONEROW + 128]

    def reduce_global_max(self, slots_scales):
        nsl = len(slots_scales)
        for i, (col, _) in enumerate(slots_scales):
            def _g(q):
                self.w(q, "cV")
                self.inc("qG", q.dma_start(out=self.smax[0:1, :], in_=self.rmax[:, col:col + 1]), 16)
            self.on("gpsimd", _g)

            def _v(v):
                self.w(v, "qG")
                self.inc("cV", v.reduce_max(out=self.sc1[0:1, i:i + 1], in_=self.smax[0:1, :],
                                            axis=AXX, apply_absolute_value=False))
            self.on("vector", _v)

        def _send(q):
            self.w(q, "cV")
            self.inc("qG", q.dma_start(out=self.g["cc_in"].ap()[:, :], in_=self.sc1[0:1, 0:64]), 16)
            self.w(q, "qG")
            self.inc("cc", q.collective_compute(
                "AllReduce", ALU.max, replica_groups=[list(range(NCORE))],
                ins=[self.g["cc_in"].ap().opt()], outs=[self.g["cc_out"].ap().opt()]))
            self.w(q, "cc")
            self.inc("qG", q.dma_start(out=self.sc1[0:1, 0:64], in_=self.g["cc_out"].ap()[:, :]), 16)
        self.on("gpsimd", _send)

        def _v2(v):
            self.w(v, "qG")
            for i, (_, dsts) in enumerate(slots_scales):
                self.fence(v, "cV", v.tensor_scalar(
                    out=self.sc1[0:1, 8 + i:9 + i], in0=self.sc1[0:1, i:i + 1],
                    scalar1=1.0 / QA, scalar2=1e-12, op0=ALU.mult, op1=ALU.max))
                v.reciprocal(out=self.sc1[0:1, 16 + i:17 + i], in_=self.sc1[0:1, 8 + i:9 + i])
                v.tensor_scalar_mul(out=self.sc1[0:1, 24 + i:25 + i],
                                    in0=self.sc1[0:1, 8 + i:9 + i], scalar1=0.5)
            self.inc("cV", v.tensor_copy(out=self.sc1[0:1, 32:33], in_=self.sc1[0:1, 0:1]))
        self.on("vector", _v2)

        def _b(q):
            self.w(q, "cV")
            self.inc("qG", q.dma_start(out=self.g["s_tmp1"][:, :], in_=self.sc1[0:1, 0:64]), 16)
            self.w(q, "qG")
            for i, (_, dsts) in enumerate(slots_scales):
                for (dst, kind) in dsts:
                    src = {"inv": 16 + i, "val": 8 + i, "half": 24 + i}[kind]
                    self.inc("qG", q.dma_start(
                        out=self.scales[:, dst:dst + 1],
                        in_=self.g["s_tmp1"][0:1, src:src + 1].broadcast_to([128, 1])), 16)
        self.on("gpsimd", _b)
        self.barrier()

    def ln_pass(self, src, chunks, apply_fn, src_off=0):
        po_x = 29000
        po_sq = po_x + 8 * TCH
        po_r = po_sq + 8 * TCH
        for (t0, tn) in chunks:
            xk = [self.pool[:, po_x + k * TCH: po_x + k * TCH + tn] for k in range(8)]
            sq8 = [self.pool[:, po_sq + i * TCH:po_sq + i * TCH + tn] for i in range(8)]
            row_m = self.pool[0:1, po_r:po_r + tn]
            row_v = self.pool[0:1, po_r + TCH:po_r + TCH + tn]
            row_b = self.pool[0:1, po_r + 2 * TCH:po_r + 2 * TCH + tn]

            def _ld(q):
                self.w(q, "cV", "cT")
                for k in range(8):
                    self.inc("qL", q.dma_start(
                        out=xk[k], in_=src[k * 128:(k + 1) * 128, src_off + t0:src_off + t0 + tn]), 16)
            self.on("sync", _ld)

            def _sq(s):
                self.w(s, "qL", "cT", "cV")
                for k in range(8):
                    ii = s.activation(out=sq8[k], in_=xk[k], func=AF.Square)
                self.inc("cS", ii)
            self.on("scalar", _sq)
            self.pe_guard("qL", "cS")

            def _te(t):
                for k in range(8):
                    t.matmul(self.ps_st[0][0:1, 0:tn], self.smc(OF_ONE), xk[k],
                             start=(k == 0), stop=(k == 7))
                    mi = t.matmul(self.ps_st[1][0:1, 0:tn], self.smc(OF_ONE), sq8[k],
                                  start=(k == 0), stop=(k == 7))
                self.inc("cT", mi)
            self.on("tensor", _te)

            def _vs(v):
                self.w(v, "cT")
                v.tensor_scalar_mul(out=row_m, in0=self.ps_st[0][0:1, 0:tn], scalar1=1.0 / D)
                v.tensor_scalar_mul(out=row_v, in0=self.ps_st[1][0:1, 0:tn], scalar1=1.0 / D)
                v.tensor_tensor(out=row_b, in0=row_m, in1=row_m, op=ALU.mult)
                v.tensor_tensor(out=row_v, in0=row_v, in1=row_b, op=ALU.subtract)
                self.inc("cV", v.tensor_scalar_add(out=row_v, in0=row_v, scalar1=EPS))
            self.on("vector", _vs)

            def _ss(s):
                self.w(s, "cV")
                self.inc("cS", s.activation(out=row_v, in_=row_v, func=AF.Sqrt))
            self.on("scalar", _ss)

            def _vr(v):
                self.w(v, "cS")
                v.reciprocal(out=row_v, in_=row_v)
                v.tensor_tensor(out=row_b, in0=row_m, in1=row_v, op=ALU.mult)
                self.inc("cV", v.tensor_scalar_mul(out=row_b, in0=row_b, scalar1=-1.0))
            self.on("vector", _vr)

            self.pe_guard("cV")

            def _tb(t):
                t.matmul(self.ps[2][:, 0:tn], self.onerow(), row_v, start=True, stop=True)
                self.inc("cT", t.matmul(self.ps[3][:, 0:tn], self.onerow(), row_b,
                                        start=True, stop=True))
            self.on("tensor", _tb)

            for k in range(8):
                apply_fn(k, t0, tn, xk[k], self.ps[2][:, 0:tn], self.ps[3][:, 0:tn])

    def gemm(self, wT, KD, MD, rhs_fn, chunks, evac_fn, wpool=33000, nbuf=2):
        nk = KD // 128
        evac_snap = {}
        for ot in range(MD // 128):
            wt = [self.pool[:, wpool + k * 128: wpool + (k + 1) * 128] for k in range(nk)]

            def _ld(q):
                self.w(q, "cT")
                for k in range(nk):
                    self.inc("qL", q.dma_start(
                        out=wt[k], in_=wT[k * 128:(k + 1) * 128, ot * 128:(ot + 1) * 128]), 16)
            self.on("sync", _ld)
            self.pe_guard("qL", "cV")

            for ci, (t0, tn) in enumerate(chunks):
                bank = self.ps[ci % 2]

                def _te(t):
                    if ci >= 2:
                        t.wait_ge(self.sem["cV"], evac_snap[ci - 2])
                    for k in range(nk):
                        mi = t.matmul(bank[:, 0:tn], wt[k], rhs_fn(k, t0, tn),
                                      start=(k == 0), stop=(k == nk - 1))
                    self.inc("cT", mi)
                self.on("tensor", _te)

                evac_fn(ot, ci, t0, tn, bank[:, 0:tn])
                evac_snap[ci] = self.cnt["cV"]

    def evac_scaled_store(self, dst, scol, mxcol, ring_off=38800):
        def _evac(ot, ci, t0, tn, bank):
            ev = self.pool[:, ring_off + (ci % 4) * TCH: ring_off + (ci % 4) * TCH + tn]
            tmp = self.pool[:, ring_off + 4 * TCH: ring_off + 4 * TCH + 1]

            def _v(v):
                self.w(v, "cT")
                if self.cnt["qS"] >= 48:
                    v.wait_ge(self.sem["qS"], self.cnt["qS"] - 48)
                v.tensor_scalar_mul(out=ev, in0=bank, scalar1=self.scales[:, scol:scol + 1])
                self.fence(v, "cV", v.reduce_max(out=tmp, in_=ev, axis=AXX,
                                                 apply_absolute_value=True))
                self.inc("cV", v.tensor_max(self.rmax[:, mxcol:mxcol + 1],
                                            self.rmax[:, mxcol:mxcol + 1], tmp))

            self.on("vector", _v)

            def _s(s):
                self.w(s, "cS") if False else None
                s.wait_ge(self.sem["cV"], self.cnt["cV"])
                self.inc("qS", s.dma_start(out=dst[ot * 128:(ot + 1) * 128, t0:t0 + tn], in_=ev), 16)
            self.on("scalar", _s)
        return _evac

    def ffn(self, x_src, res_src, dst, chunks, with_halo, tag="f"):
        g = self.g
        ttw = TT if with_halo else TCORE
        def _vz(v):
            self.inc("cV", v.memset(self.rmax[:, 0:2], 0.0))
        self.on("vector", _vz)

        def apply_ln(k, t0, tn, xk, a_ps, b_ps):
            po_o = 39700
            ot_ = self.pool[:, po_o + (k % 2) * TCH: po_o + (k % 2) * TCH + tn]
            tmp = self.pool[:, po_o + 2 * TCH: po_o + 2 * TCH + 1]

            def _v(v):
                self.w(v, "cT", "qS")
                v.tensor_tensor(out=ot_, in0=xk, in1=a_ps, op=ALU.mult)
                v.tensor_tensor(out=ot_, in0=ot_, in1=b_ps, op=ALU.add)
                self.fence(v, "cV", v.reduce_max(out=tmp, in_=ot_, axis=AXX,
                                                 apply_absolute_value=True))
                self.inc("cV", v.tensor_max(self.rmax[:, 0:1], self.rmax[:, 0:1], tmp))
            self.on("vector", _v)

            def _s(s):
                s.wait_ge(self.sem["cV"], self.cnt["cV"])
                self.inc("qS", s.dma_start(out=g["s_xln"][k * 128:(k + 1) * 128, t0:t0 + tn],
                                           in_=ot_), 16)
            self.on("scalar", _s)

        self.ln_pass(x_src, chunks, apply_ln)
        self.barrier()
        self.dbg("xln_" + tag, g["s_xln"][:, :], [D, TT])
        self.reduce_global_max([(0, [(SC_XI, "inv"), (SC_X, "val")])])

        PO_XI = 0
        for k in range(8):
            xi_k = self.pool[:, PO_XI + k * TT: PO_XI + k * TT + ttw]

            def _ld(q):
                self.inc("qL", q.dma_start(out=xi_k, in_=g["s_xln"][k * 128:(k + 1) * 128, 0:ttw]), 16)
            self.on("sync", _ld)

            def _v(v):
                self.w(v, "qL")
                v.tensor_scalar(out=xi_k, in0=xi_k, scalar1=self.scales[:, SC_XI:SC_XI + 1],
                                scalar2=MAGIC, op0=ALU.mult, op1=ALU.add)
                v.tensor_scalar(out=xi_k, in0=xi_k, scalar1=MAGIC, scalar2=QA,
                                op0=ALU.subtract, op1=ALU.min)
                self.inc("cV", v.tensor_scalar_max(out=xi_k, in0=xi_k, scalar1=-QA - 1.0))
            self.on("vector", _v)
        self.dbg("xi_" + tag, self.pool[:, 0:8 * TT], [128, 8 * TT])
        self.barrier()

        def _vz2(v):
            self.inc("cV", v.memset(self.rmax[:, 0:2], 0.0))
        self.on("vector", _vz2)
        rhs_xi = lambda k, t0, tn: self.pool[:, PO_XI + k * TT + t0: PO_XI + k * TT + t0 + tn]
        self.gemm(g["s_wupT"], D, H, rhs_xi, chunks,
                  self.evac_scaled_store(g["s_up"], SC_X, 0))
        self.gemm(g["s_wgateT"], D, H, rhs_xi, chunks,
                  self.evac_scaled_store(g["s_gate"], SC_X, 1))
        self.barrier()
        self.dbg("up_" + tag, g["s_up"][:, :], [H, TT])
        self.dbg("gate_" + tag, g["s_gate"][:, :], [H, TT])
        self.reduce_global_max([(0, [(SC_UI, "inv"), (SC_U, "val")]),
                                (1, [(SC_GI, "inv"), (SC_G, "val")])])

        def _vz3(v):
            self.inc("cV", v.memset(self.rmax[:, 0:1], 0.0))
        self.on("vector", _vz3)
        PO_G, PO_SIG, PO_T = 0, 4200, 8400
        for ot in range(H // 128):
            grow = self.pool[:, PO_G:PO_G + ttw]
            sig = self.pool[:, PO_SIG:PO_SIG + ttw]
            tmp1 = self.pool[:, PO_T:PO_T + 1]

            def _ld(q):
                self.w(q, "qS", "cS", "cV")
                self.inc("qL", q.dma_start(out=grow, in_=g["s_gate"][ot * 128:(ot + 1) * 128, 0:ttw]), 16)
            self.on("sync", _ld)

            def _v(v):
                self.w(v, "qL")
                v.tensor_scalar(out=grow, in0=grow, scalar1=self.scales[:, SC_GI:SC_GI + 1],
                                scalar2=MAGIC, op0=ALU.mult, op1=ALU.add)
                v.tensor_scalar(out=grow, in0=grow, scalar1=MAGIC, scalar2=QA,
                                op0=ALU.subtract, op1=ALU.min)
                v.tensor_scalar_max(out=grow, in0=grow, scalar1=-QA - 1.0)
                self.inc("cV", v.tensor_scalar_mul(out=grow, in0=grow,
                                                   scalar1=self.scales[:, SC_G:SC_G + 1]))
            self.on("vector", _v)

            def _s(s):
                self.w(s, "cV")
                self.inc("cS", s.activation(out=sig, in_=grow, func=AF.Sigmoid))
            self.on("scalar", _s)

            def _v2(v):
                self.w(v, "cS")
                v.tensor_scalar(out=sig, in0=sig, scalar1=65536.0, scalar2=MAGIC,
                                op0=ALU.mult, op1=ALU.add)
                v.tensor_scalar(out=sig, in0=sig, scalar1=MAGIC, scalar2=65535.0,
                                op0=ALU.subtract, op1=ALU.min)
                v.tensor_scalar(out=sig, in0=sig, scalar1=0.0, scalar2=1.0 / 65536.0,
                                op0=ALU.max, op1=ALU.mult)
                v.tensor_tensor(out=grow, in0=grow, in1=sig, op=ALU.mult)
                self.fence(v, "cV", v.reduce_max(out=tmp1, in_=grow, axis=AXX,
                                                 apply_absolute_value=True))
                self.inc("cV", v.tensor_max(self.rmax[:, 0:1], self.rmax[:, 0:1], tmp1))
            self.on("vector", _v2)

            def _st(s):
                self.w(s, "cV")
                self.inc("qS", s.dma_start(out=g["s_gs"][ot * 128:(ot + 1) * 128, 0:ttw], in_=grow), 16)
            self.on("scalar", _st)
        self.barrier()
        self.dbg("gs_" + tag, g["s_gs"][:, :], [H, TT])
        self.reduce_global_max([(0, [(SC_GSI, "inv"), (SC_GS, "val")])])

        def _vz4(v):
            self.inc("cV", v.memset(self.rmax[:, 0:1], 0.0))
        self.on("vector", _vz4)
        for ot in range(H // 128):
            gsr = self.pool[:, PO_G:PO_G + ttw]
            upr = self.pool[:, PO_SIG:PO_SIG + ttw]
            tmp1 = self.pool[:, PO_T:PO_T + 1]

            def _ld(q):
                self.w(q, "qS", "cV")
                self.inc("qL", q.dma_start(out=gsr, in_=g["s_gs"][ot * 128:(ot + 1) * 128, 0:ttw]), 16)
                self.inc("qL", q.dma_start(out=upr, in_=g["s_up"][ot * 128:(ot + 1) * 128, 0:ttw]), 16)
            self.on("sync", _ld)

            def _v(v):
                self.w(v, "qL")
                v.tensor_scalar(out=gsr, in0=gsr, scalar1=self.scales[:, SC_GSI:SC_GSI + 1],
                                scalar2=MAGIC, op0=ALU.mult, op1=ALU.add)
                v.tensor_scalar(out=gsr, in0=gsr, scalar1=MAGIC, scalar2=QA,
                                op0=ALU.subtract, op1=ALU.min)
                v.tensor_scalar_max(out=gsr, in0=gsr, scalar1=-QA - 1.0)
                v.tensor_scalar_mul(out=gsr, in0=gsr, scalar1=self.scales[:, SC_GS:SC_GS + 1])
                v.tensor_scalar(out=upr, in0=upr, scalar1=self.scales[:, SC_UI:SC_UI + 1],
                                scalar2=MAGIC, op0=ALU.mult, op1=ALU.add)
                v.tensor_scalar(out=upr, in0=upr, scalar1=MAGIC, scalar2=QA,
                                op0=ALU.subtract, op1=ALU.min)
                v.tensor_scalar_max(out=upr, in0=upr, scalar1=-QA - 1.0)
                v.tensor_scalar_mul(out=upr, in0=upr, scalar1=self.scales[:, SC_U:SC_U + 1])
                v.tensor_tensor(out=gsr, in0=gsr, in1=upr, op=ALU.mult)
                self.fence(v, "cV", v.reduce_max(out=tmp1, in_=gsr, axis=AXX,
                                                 apply_absolute_value=True))
                self.inc("cV", v.tensor_max(self.rmax[:, 0:1], self.rmax[:, 0:1], tmp1))
            self.on("vector", _v)

            def _st(s):
                self.w(s, "cV")
                self.inc("qS", s.dma_start(out=g["s_gate"][ot * 128:(ot + 1) * 128, 0:ttw], in_=gsr), 16)
            self.on("scalar", _st)
        self.barrier()
        self.dbg("opre_" + tag, g["s_gate"][:, :], [H, TT])
        self.reduce_global_max([(0, [(SC_OI, "inv"), (SC_OH, "half")])])

        if with_halo:
            big = [(0, 1024), (1024, 1024), (2048, 1024), (3072, 1024), (TT - 1024, 1024)]
        else:
            big = t_chunks(1024, TCORE)
        PO_OI = 0
        PO_W3 = 26000
        PO_EV = 30000
        for (bt0, btn) in big:
            oik = [self.pool[:, PO_OI + k * 1024: PO_OI + k * 1024 + btn] for k in range(24)]

            def _ld(q):
                self.w(q, "cT", "cV")
                for k in range(24):
                    self.inc("qL", q.dma_start(
                        out=oik[k], in_=g["s_gate"][k * 128:(k + 1) * 128, bt0:bt0 + btn]), 16)
            self.on("sync", _ld)

            for k in range(24):
                def _v(v, k=k):
                    self.w(v, "qL")
                    v.tensor_scalar(out=oik[k], in0=oik[k], scalar1=self.scales[:, SC_OI:SC_OI + 1],
                                    scalar2=MAGIC, op0=ALU.mult, op1=ALU.add)
                    v.tensor_scalar(out=oik[k], in0=oik[k], scalar1=MAGIC, scalar2=QA,
                                    op0=ALU.subtract, op1=ALU.min)
                    self.inc("cV", v.tensor_scalar_max(out=oik[k], in0=oik[k], scalar1=-QA - 1.0))
                self.on("vector", _v)
            qsnap = self.cnt["cV"]

            pieces = t_chunks(512, btn, bt0)
            for dt_ in range(8):
                wt = [self.pool[:, PO_W3 + k * 128: PO_W3 + (k + 1) * 128] for k in range(24)]

                def _lw(q):
                    self.w(q, "cT")
                    for k in range(24):
                        self.inc("qL", q.dma_start(
                            out=wt[k], in_=g["s_wdownT"][k * 128:(k + 1) * 128,
                                                         dt_ * 128:(dt_ + 1) * 128]), 16)
                self.on("sync", _lw)
                self.pe_guard("qL", "cV")

                for pi, (pt0, ptn) in enumerate(pieces):
                    bank = self.ps[pi % 2]

                    def _te(t):
                        t.wait_ge(self.sem["cV"], self.cnt["cV"])
                        for k in range(24):
                            mi = t.matmul(bank[:, 0:ptn],
                                          wt[k], oik[k][:, pt0 - bt0:pt0 - bt0 + ptn],
                                          start=(k == 0), stop=(k == 23))
                        self.inc("cT", mi)
                    self.on("tensor", _te)

                    xr = self.pool[:, PO_EV + (pi % 2) * 512: PO_EV + (pi % 2) * 512 + ptn]
                    ev = self.pool[:, PO_EV + 1024 + (pi % 2) * 512: PO_EV + 1024 + (pi % 2) * 512 + ptn]

                    def _lx(q):
                        self.w(q, "cV")
                        self.inc("qL", q.dma_start(
                            out=xr, in_=res_src[dt_ * 128:(dt_ + 1) * 128, pt0:pt0 + ptn]), 16)
                    self.on("sync", _lx)

                    def _v(v):
                        self.w(v, "cT", "qL", "qS")
                        self.inc("cV", v.scalar_tensor_tensor(
                            out=ev, in0=bank[:, 0:ptn], scalar=self.scales[:, SC_OH:SC_OH + 1],
                            in1=xr, op0=ALU.mult, op1=ALU.add))
                    self.on("vector", _v)

                    def _st(s):
                        s.wait_ge(self.sem["cV"], self.cnt["cV"])
                        self.inc("qS", s.dma_start(out=dst[dt_ * 128:(dt_ + 1) * 128, pt0:pt0 + ptn],
                                                   in_=ev), 16)
                    self.on("scalar", _st)
        self.dbg("scales_" + tag, self.scales[:, :], [128, 24])
        self.barrier()

    def s0_consts(self):
        g = self.g

        def _sy(q):
            self.inc("qL", q.dma_start(out=self.sm[:], in_=g["smalls"][:, :]), 16)
            self.inc("qL", q.dma_start(out=self.ident[:], in_=g["ident"][:, :]), 16)
        self.on("sync", _sy)

        def _se(s):
            self.w(s, "qL")
            s.activation(out=self.a_scan[:], in_=self.sm[:, OF_A:OF_A + 256], func=AF.Exp)
            self.inc("cS", s.mul(out=self.a_scan[:], in_=self.a_scan[:], mul=-1.0))
        self.on("scalar", _se)

        def _vz(v):
            self.w(v, "qL")
            v.memset(self.sc1[:, :], 0.0)
            v.memset(self.smax[:, :], 0.0)
            for p in self.ps:
                v.memset(p[:, :], 0.0)
            for p in self.ps_st:
                v.memset(p[:, :], 0.0)
            self.inc("cV", v.memset(self.rmax[:, :], 0.0))
        self.on("vector", _vz)
        for _ in range(3):
            self.pe_guard("cV")
        self.barrier()

    def s1_dequant(self):
        g = self.g
        self._s1_snap = []
        for (w_rm, w_t, O, I) in ((g["w_up"], g["s_wupT"], H, D),
                                  (g["w_gate"], g["s_wgateT"], H, D),
                                  (g["w_down"], g["s_wdownT"], D, H)):
            nblk = I // BLK
            wrow = self.pool[:, 0:I]
            stile = self.pool[:, 4096:4096 + nblk]
            rtile = self.pool[:, 4352:4352 + nblk]
            qtile = self.pool[:, 4608:4608 + I]
            for ot in range(O // 128):
                def _ld(q):
                    self.w(q, "cT")
                    self.inc("qL", q.dma_start(out=wrow, in_=w_rm[ot * 128:(ot + 1) * 128, :]), 16)
                self.on("sync", _ld)

                def _ve(v):
                    self.w(v, "qL", "cT")
                    self.fence(v, "cV", v.reduce_max(
                        out=stile, in_=wrow.rearrange("p (b e) -> p b e", e=BLK),
                        axis=AXX, apply_absolute_value=True))
                    self.fence(v, "cV", v.tensor_scalar(
                        out=stile, in0=stile, scalar1=1.0 / WQ, scalar2=1e-12,
                        op0=ALU.mult, op1=ALU.max))
                    self.fence(v, "cV", v.reciprocal(out=rtile, in_=stile))
                    rb = rtile.rearrange("p (b e) -> p b e", e=1).broadcast_to([128, nblk, BLK])
                    v.tensor_tensor(out=qtile.rearrange("p (b e) -> p b e", e=BLK),
                                    in0=wrow.rearrange("p (b e) -> p b e", e=BLK), in1=rb,
                                    op=ALU.mult)
                    v.tensor_scalar(out=qtile, in0=qtile, scalar1=MAGIC, scalar2=MAGIC,
                                    op0=ALU.add, op1=ALU.subtract)
                    v.tensor_scalar(out=qtile, in0=qtile, scalar1=WQ, scalar2=-WQ - 1.0,
                                    op0=ALU.min, op1=ALU.max)
                    sb = stile.rearrange("p (b e) -> p b e", e=1).broadcast_to([128, nblk, BLK])
                    self.inc("cV", v.tensor_tensor(
                        out=qtile.rearrange("p (b e) -> p b e", e=BLK),
                        in0=qtile.rearrange("p (b e) -> p b e", e=BLK), in1=sb, op=ALU.mult))
                self.on("vector", _ve)
                if DEBUG and ot == 0:
                    self.dbg("qtile_%s_ot0" % ("up" if w_rm is g["w_up"] else
                                               ("gate" if w_rm is g["w_gate"] else "down")),
                             qtile, [128, I])
                self.pe_guard("cV", "cS")

                for it in range(I // 128):
                    bank = self.ps[it % 2]

                    def _te(t, it=it, bank=bank):
                        self.w(t, "cV")
                        snaps = self._s1_snap
                        if len(snaps) >= 2:
                            t.wait_ge(self.sem["cS"], snaps[-2])
                        self.inc("cT", t.matmul(bank[:, 0:128],
                                                qtile[:, it * 128:(it + 1) * 128],
                                                self.ident[:], start=True, stop=True))
                    self.on("tensor", _te)

                    def _sc(s, it=it, ot=ot, bank=bank):
                        self.w(s, "cT")
                        ev = self.pool[:, 12000 + (it % 2) * 128: 12000 + (it % 2) * 128 + 128]
                        if self.cnt["qS"] >= 32:
                            s.wait_ge(self.sem["qS"], self.cnt["qS"] - 16)
                        self.inc("cS", s.copy(out=ev, in_=bank[:, 0:128]))
                        self._s1_snap.append(self.cnt["cS"])
                        self.inc("qS", s.dma_start(
                            out=w_t[it * 128:(it + 1) * 128, ot * 128:(ot + 1) * 128], in_=ev), 16)
                    self.on("scalar", _sc)
        self.dbg("wupT", self.g["s_wupT"][:, :], [D, H])
        self.dbg("wgateT", self.g["s_wgateT"][:, :], [D, H])
        self.dbg("wdownT", self.g["s_wdownT"][:, :], [H, D])
        self.barrier()

    def mamba(self):
        g = self.g
        PO_XL = 0

        def apply_ln2(k, t0, tn, xk, a_ps, b_ps):
            dst = self.pool[:, PO_XL + k * TT + t0: PO_XL + k * TT + t0 + tn]

            def _v(v):
                self.w(v, "cT")
                v.tensor_tensor(out=dst, in0=xk, in1=a_ps, op=ALU.mult)
                self.inc("cV", v.tensor_tensor(out=dst, in0=dst, in1=b_ps, op=ALU.add))
            self.on("vector", _v)

        self.ln_pass(g["s_xres1"], tt_chunks(), apply_ln2)
        self.barrier()

        rhs_xl = lambda k, t0, tn: self.pool[:, PO_XL + k * TT + t0: PO_XL + k * TT + t0 + tn]

        def evac_inproj(ot, ci, t0, tn, bank):
            ev = self.pool[:, 38800 + (ci % 3) * TCH: 38800 + (ci % 3) * TCH + tn]
            if ot < 16:
                def _v(v):
                    self.w(v, "cT", "qS")
                    self.inc("cV", v.tensor_copy(out=ev, in_=bank))
                self.on("vector", _v)

                def _s(s):
                    s.wait_ge(self.sem["cV"], self.cnt["cV"])
                    self.inc("qS", s.dma_start(
                        out=g["s_xc"][ot * 128:(ot + 1) * 128, t0:t0 + tn], in_=ev), 16)
                self.on("scalar", _s)
            else:
                def _s(s):
                    self.w(s, "cT", "qS")
                    self.inc("cS", s.activation(out=ev, in_=bank, func=AF.Silu))
                    self.inc("qS", s.dma_start(
                        out=g["s_sz"][(ot - 16) * 128:(ot - 15) * 128, t0:t0 + tn], in_=ev), 16)
                self.on("scalar", _s)

            def _vev(v):
                self.inc("cV", v.engine_nop()) if False else None
            if ot >= 16:
                def _vdummy(v):
                    self.w(v, "cS")
                    self.inc("cV", v.engine_nop())
                self.on("vector", _vdummy)

        self.gemm(g["w_inT"], D, 2 * E, rhs_xl, tt_chunks(), evac_inproj)
        self.barrier()

        for et in range(16):
            xcr = self.pool[:, 0:TT]
            acc = self.pool[:, 4200:4200 + TCORE]

            def _ld(q):
                self.w(q, "qS", "cS", "cV")
                self.inc("qL", q.dma_start(out=xcr, in_=g["s_xc"][et * 128:(et + 1) * 128, :]), 16)
            self.on("sync", _ld)

            def _v(v):
                self.w(v, "qL")
                v.tensor_scalar(out=acc, in0=xcr[:, 0:TCORE],
                                scalar1=self.smc(OF_CW + et * 4), scalar2=self.smc(OF_CB + et),
                                op0=ALU.mult, op1=ALU.add)
                for k in range(1, 4):
                    v.scalar_tensor_tensor(out=acc, in0=xcr[:, k:k + TCORE],
                                           scalar=self.smc(OF_CW + et * 4 + k), in1=acc,
                                           op0=ALU.mult, op1=ALU.add)
                self.inc("cV", v.tensor_copy(out=acc[:, 0:1], in_=acc[:, 0:1]))
            self.on("vector", _v)

            def _s(s):
                self.w(s, "cV")
                self.inc("cS", s.activation(out=acc, in_=acc, func=AF.Silu))
                self.inc("qS", s.dma_start(out=g["s_xa"][et * 128:(et + 1) * 128, :], in_=acc), 16)
            self.on("scalar", _s)
        self.barrier()

        PO_XA, PO_WXP, PO_WDT, PO_DLT, PO_TMP = 0, 8400, 11000, 13100, 13800

        def _lw(q):
            for et in range(16):
                self.inc("qL", q.dma_start(
                    out=self.pool[:, PO_WXP + et * 128: PO_WXP + (et + 1) * 128],
                    in_=g["w_xpT"][et * 128:(et + 1) * 128, :]), 16)
            for ec in range(16):
                self.inc("qL", q.dma_start(
                    out=self.pool[0:RNK, PO_WDT + ec * 128: PO_WDT + (ec + 1) * 128],
                    in_=g["w_dtT"][:, ec * 128:(ec + 1) * 128]), 16)
        self.on("sync", _lw)

        def _vz(v):
            self.inc("cV", v.memset(self.sumdelta[:, :], 0.0))
        self.on("vector", _vz)

        for ci, (t0, tn) in enumerate(t_chunks(SCH, TCORE)):
            xak = [self.pool[:, PO_XA + k * SCH: PO_XA + k * SCH + tn] for k in range(16)]

            def _ld(q):
                self.w(q, "cT")
                for k in range(16):
                    self.inc("qL", q.dma_start(
                        out=xak[k], in_=g["s_xa"][k * 128:(k + 1) * 128, t0:t0 + tn]), 16)
            self.on("sync", _ld)
            self.pe_guard("qL", "cV", "cS")

            def _te(t):
                for k in range(16):
                    mi = t.matmul(self.ps[4][:, 0:tn],
                                  self.pool[:, PO_WXP + k * 128: PO_WXP + (k + 1) * 128],
                                  xak[k], start=(k == 0), stop=(k == 15))
                self.inc("cT", mi)
            self.on("tensor", _te)

            dlt = self.pool[0:RNK, PO_DLT:PO_DLT + tn]

            def _v(v):
                self.w(v, "cT")
                self.inc("cV", v.tensor_copy(out=dlt, in_=self.ps[4][0:RNK, 0:tn]))
            self.on("vector", _v)

            def _sbc(s):
                self.w(s, "cT")
                bts = self.pool[0:NST, PO_TMP:PO_TMP + tn]
                cts = self.pool[0:NST, PO_TMP + 600:PO_TMP + 600 + tn]
                s.copy(out=bts, in_=self.ps[4][64:80, 0:tn])
                self.inc("cS", s.copy(out=cts, in_=self.ps[4][96:112, 0:tn]))
                self.inc("qS", s.dma_start(out=g["s_bt"][ci, :, 0:tn], in_=bts), 16)
                self.inc("qS", s.dma_start(out=g["s_ct"][ci, :, 0:tn], in_=cts), 16)
            self.on("scalar", _sbc)
            self.pe_guard("cV", "cS")

            for ec in range(16):
                bank = self.ps[5]

                def _t2(t, ec=ec):
                    self.w(t, "cV", "cS")
                    self.inc("cT", t.matmul(bank[:, 0:tn],
                                            self.pool[0:RNK, PO_WDT + ec * 128: PO_WDT + (ec + 1) * 128],
                                            dlt, start=True, stop=True))
                self.on("tensor", _t2)

                dts = self.pool[:, PO_TMP + 1200:PO_TMP + 1200 + tn]
                dexp = self.pool[:, PO_TMP + 1800:PO_TMP + 1800 + tn]
                tred = self.pool[:, PO_TMP + 2400:PO_TMP + 2401]

                def _v2(v, ec=ec):
                    self.w(v, "cT")
                    self.inc("cV", v.tensor_scalar(out=dts, in0=bank[:, 0:tn],
                                                   scalar1=self.smc(OF_DTB + ec), scalar2=42.0,
                                                   op0=ALU.add, op1=ALU.min))
                self.on("vector", _v2)

                def _s2(s):
                    self.w(s, "cV")
                    s.activation(out=dexp, in_=dts, func=AF.Exp)
                    self.inc("cS", s.activation(out=dexp, in_=dexp, func=AF.Ln, bias=1.0))
                self.on("scalar", _s2)

                def _v3(v, ec=ec):
                    self.w(v, "cS")
                    v.scalar_tensor_tensor(out=dexp, in0=bank[:, 0:tn],
                                           scalar=self.smc(OF_DTB + ec), in1=dexp,
                                           op0=ALU.add, op1=ALU.max)
                    self.fence(v, "cV", v.reduce_sum(out=tred, in_=dexp, axis=AXX))
                    self.inc("cV", v.tensor_tensor(out=self.sumdelta[:, ec:ec + 1],
                                                   in0=self.sumdelta[:, ec:ec + 1], in1=tred,
                                                   op=ALU.add))
                self.on("vector", _v3)

                def _s3(s, ec=ec):
                    self.w(s, "cV")
                    self.inc("qS", s.dma_start(
                        out=g["s_delta"][ec * 128:(ec + 1) * 128, t0:t0 + tn], in_=dexp), 16)
                self.on("scalar", _s3)
        self.barrier()

        def _sp(s):
            for es in range(16):
                for n in range(NST):
                    gcol = es * 16 + n
                    ii = s.activation(out=self.p_sb[:, gcol:gcol + 1],
                                      in_=self.sumdelta[:, es:es + 1], func=AF.Exp,
                                      scale=self.a_scan[:, gcol:gcol + 1])
            self.inc("cS", ii)
        self.on("scalar", _sp)

        def _vc(v):
            self.inc("cV", v.memset(self.carry[:, :], 0.0))
        self.on("vector", _vc)
        self.barrier()

        self.scan_pass(compute_y=False)
        self.dbg("S1", self.carry[:, :], [128, 256])
        self.dbg("P1", self.p_sb[:, :], [128, 256])
        self.barrier()

        def _ag(q):
            self.inc("qG", q.dma_start(out=g["ag_in"].ap()[:, 0:256], in_=self.p_sb[:, :]), 16)
            self.inc("qG", q.dma_start(out=g["ag_in"].ap()[:, 256:512], in_=self.carry[:, :]), 16)
            self.w(q, "qG")
            self.inc("cc", q.collective_compute(
                "AllGather", ALU.bypass, replica_groups=[[0, 1, 2, 3], [4, 5, 6, 7]],
                ins=[g["ag_in"].ap().opt()], outs=[g["ag_out"].ap().opt()]))
            self.w(q, "cc")
        self.on("gpsimd", _ag)

        def _vh(v):
            self.inc("cV", v.memset(self.h_in[:, :], 0.0))
        self.on("vector", _vh)
        for j in range(3):
            def _lg(q, j=j):
                self.w(q, "cc", "cV")
                self.inc("qG", q.dma_start(out=self.agl[:, :],
                                           in_=g["ag_out"].ap()[j * 128:(j + 1) * 128, :]), 16)
            self.on("gpsimd", _lg)

            def _vj(v, j=j):
                self.w(v, "qG")
                pj = self.agl[:, 0:256]
                sj = self.agl[:, 256:512]
                aj = self.pool[:, 0:256]
                self.fence(v, "cV", v.tensor_scalar(out=aj, in0=pj, scalar1=self.smc(OF_U + j),
                                                    scalar2=self.smc(OF_V + j),
                                                    op0=ALU.mult, op1=ALU.add))
                self.fence(v, "cV", v.tensor_tensor(out=self.h_in[:, :], in0=self.h_in[:, :],
                                                    in1=aj, op=ALU.mult))
                self.inc("cV", v.scalar_tensor_tensor(out=self.h_in[:, :], in0=sj,
                                                      scalar=self.smc(OF_U + j), in1=self.h_in[:, :],
                                                      op0=ALU.mult, op1=ALU.add))
            self.on("vector", _vj)
        def _vcc(v):
            self.inc("cV", v.tensor_copy(out=self.carry[:, :], in_=self.h_in[:, :]))
        self.on("vector", _vcc)
        self.barrier()

        self.scan_pass(compute_y=True)
        self.barrier()

        PO_YG, PO_WO, PO_EV = 0, 20000, 26000
        for ci, (t0, tn) in enumerate(t_chunks(SCH, TCORE)):
            ygk = [self.pool[:, PO_YG + k * SCH: PO_YG + k * SCH + tn] for k in range(16)]

            def _ld(q):
                self.w(q, "cT")
                for k in range(16):
                    self.inc("qL", q.dma_start(
                        out=ygk[k], in_=g["s_yg"][k * 128:(k + 1) * 128, t0:t0 + tn]), 16)
            self.on("sync", _ld)

            for dt_ in range(8):
                wt = [self.pool[:, PO_WO + k * 128: PO_WO + (k + 1) * 128] for k in range(16)]

                def _lw(q, dt_=dt_):
                    self.w(q, "cT")
                    for k in range(16):
                        self.inc("qL", q.dma_start(
                            out=wt[k], in_=g["w_outT"][k * 128:(k + 1) * 128,
                                                       dt_ * 128:(dt_ + 1) * 128]), 16)
                self.on("sync", _lw)
                self.pe_guard("qL", "cV")

                bank = self.ps[dt_ % 2]

                def _te(t, bank=bank):
                    for k in range(16):
                        mi = t.matmul(bank[:, 0:tn], wt[k], ygk[k], start=(k == 0), stop=(k == 15))
                    self.inc("cT", mi)
                self.on("tensor", _te)

                xr = self.pool[:, PO_EV + (dt_ % 2) * 512: PO_EV + (dt_ % 2) * 512 + tn]
                ev = self.pool[:, PO_EV + 1024 + (dt_ % 2) * 512: PO_EV + 1024 + (dt_ % 2) * 512 + tn]

                def _lx(q, dt_=dt_):
                    self.w(q, "cV")
                    self.inc("qL", q.dma_start(
                        out=xr, in_=g["s_xres1"][dt_ * 128:(dt_ + 1) * 128,
                                                 HALO + t0:HALO + t0 + tn]), 16)
                self.on("sync", _lx)

                def _v(v, bank=bank):
                    self.w(v, "cT", "qL", "qS")
                    self.inc("cV", v.tensor_tensor(out=ev, in0=bank[:, 0:tn], in1=xr, op=ALU.add))
                self.on("vector", _v)

                def _st(s, dt_=dt_):
                    s.wait_ge(self.sem["cV"], self.cnt["cV"])
                    self.inc("qS", s.dma_start(
                        out=g["s_xres2"][dt_ * 128:(dt_ + 1) * 128, t0:t0 + tn], in_=ev), 16)
                self.on("scalar", _st)
        self.barrier()

    def scan_pass(self, compute_y):
        g = self.g
        PB, PC, PU, PH, PDE, PXA, PSZ, PDX, PDA, PYG = (
            0, 8192, 16384, 24576, 32768, 33280, 33792, 34304, 34816, 35840)
        for ci in range(NSCH):
            t0 = ci * SCH

            def _bc(q):
                self.w(q, "cV")
                self.inc("qG", q.dma_start(
                    out=self.pool[:, PB:PB + NST * SCH],
                    in_=g["s_bt"][ci].rearrange("n t -> (n t)").rearrange("(a f) -> a f", a=1)
                        .broadcast_to([128, NST * SCH])), 16)
                if compute_y:
                    self.inc("qG", q.dma_start(
                        out=self.pool[:, PC:PC + NST * SCH],
                        in_=g["s_ct"][ci].rearrange("n t -> (n t)").rearrange("(a f) -> a f", a=1)
                            .broadcast_to([128, NST * SCH])), 16)
            self.on("gpsimd", _bc)

            for es in range(16):
                des = self.pool[:, PDE:PDE + SCH]
                xas = self.pool[:, PXA:PXA + SCH]
                szs = self.pool[:, PSZ:PSZ + SCH]
                dxa = self.pool[:, PDX:PDX + SCH]

                def _ld(q, es=es):
                    self.w(q, "cV", "cS")
                    self.inc("qL", q.dma_start(
                        out=des, in_=g["s_delta"][es * 128:(es + 1) * 128, t0:t0 + SCH]), 16)
                    self.inc("qL", q.dma_start(
                        out=xas, in_=g["s_xa"][es * 128:(es + 1) * 128, t0:t0 + SCH]), 16)
                    if compute_y:
                        self.inc("qL", q.dma_start(
                            out=szs, in_=g["s_sz"][es * 128:(es + 1) * 128,
                                                   HALO + t0:HALO + t0 + SCH]), 16)
                self.on("sync", _ld)

                def _vu(v):
                    self.w(v, "qL", "qG")
                    v.tensor_tensor(out=dxa, in0=des, in1=xas, op=ALU.mult)
                    dxab = dxa.rearrange("p (a t) -> p a t", a=1).broadcast_to([128, NST, SCH])
                    self.inc("cV", v.tensor_tensor(
                        out=self.pool[:, PU:PU + NST * SCH].rearrange("p (n t) -> p n t", n=NST),
                        in0=dxab,
                        in1=self.pool[:, PB:PB + NST * SCH].rearrange("p (n t) -> p n t", n=NST),
                        op=ALU.mult))
                self.on("vector", _vu)

                exp_snap = {}
                scan_snap = {}
                for n in range(NST):
                    gcol = es * 16 + n
                    da = self.pool[:, PDA + (n % 2) * SCH: PDA + (n % 2) * SCH + SCH]

                    def _se(s, n=n, da=da):
                        if n == 0:
                            self.w(s, "cV")
                        if n >= 2:
                            s.wait_ge(self.sem["cV"], scan_snap[n - 2])
                        self.inc("cS", s.activation(out=da, in_=des, func=AF.Exp,
                                                    scale=self.a_scan[:, gcol:gcol + 1]))
                        exp_snap[n] = self.cnt["cS"]
                    self.on("scalar", _se)

                    def _vs(v, n=n, gcol=gcol, da=da):
                        v.wait_ge(self.sem["cS"], exp_snap[n])
                        hsl = self.pool[:, PH + n * SCH: PH + n * SCH + SCH]
                        self.inc("cV", v.tensor_tensor_scan(
                            out=hsl, data0=da, data1=self.pool[:, PU + n * SCH: PU + n * SCH + SCH],
                            initial=self.carry[:, gcol:gcol + 1], op0=ALU.mult, op1=ALU.add))
                        scan_snap[n] = self.cnt["cV"]
                    self.on("vector", _vs)

                def _vcar(v, es=es):
                    v.wait_ge(self.sem["cV"], scan_snap[NST - 1])
                    h3 = self.pool[:, PH:PH + NST * SCH].rearrange("p (n t) -> p n t", n=NST)
                    self.inc("cV", v.tensor_copy(
                        out=self.carry[:, es * 16:es * 16 + 16].rearrange("p (n o) -> p n o", o=1),
                        in_=h3[:, :, SCH - 1:SCH]))
                self.on("vector", _vcar)

                if compute_y:
                    def _vw(v):
                        self.inc("cV", v.tensor_tensor(
                            out=self.pool[:, PU:PU + NST * SCH],
                            in0=self.pool[:, PH:PH + NST * SCH],
                            in1=self.pool[:, PC:PC + NST * SCH], op=ALU.mult))
                    self.on("vector", _vw)
                    self.pe_guard("cV")

                    bank = self.ps[4 + (es % 2)]

                    def _ty(t, es=es, bank=bank):
                        for n in range(NST):
                            mi = t.matmul(bank[:, 0:SCH], self.ident[:],
                                          self.pool[:, PU + n * SCH: PU + n * SCH + SCH],
                                          start=(n == 0), stop=(n == NST - 1))
                        self.inc("cT", mi)
                    self.on("tensor", _ty)

                    ev = self.pool[:, PYG:PYG + SCH]

                    def _vy(v, es=es, bank=bank):
                        self.w(v, "cT")
                        if self.cnt["qS"] >= 16:
                            v.wait_ge(self.sem["qS"], self.cnt["qS"])
                        v.scalar_tensor_tensor(out=ev, in0=xas, scalar=self.smc(OF_D + es),
                                               in1=bank[:, 0:SCH], op0=ALU.mult, op1=ALU.add)
                        self.inc("cV", v.tensor_tensor(out=ev, in0=ev, in1=szs, op=ALU.mult))
                    self.on("vector", _vy)

                    def _sy(s, es=es):
                        s.wait_ge(self.sem["cV"], self.cnt["cV"])
                        self.inc("qS", s.dma_start(
                            out=g["s_yg"][es * 128:(es + 1) * 128, t0:t0 + SCH], in_=ev), 16)
                    self.on("scalar", _sy)

    def build(self):
        g = self.g
        self.s0_consts()
        self.s1_dequant()
        self.ffn(g["xT"], g["xT"], g["s_xres1"], tt_chunks(), with_halo=True, tag="f1")
        self.mamba()
        self.ffn(g["s_xres2"], g["s_xres2"], g["outT"], t_chunks(TCH, TCORE), with_halo=False, tag="f2")
        if DEBUG:
            _debug_outputs(self)
        self.barrier()
        self.es.close()
        return self.nc


def _host_inputs(L0_attn_x, L0_attn_in_proj_w, L0_attn_conv_w, L0_attn_conv_b,
                 L0_attn_x_proj_w, L0_attn_dt_proj_w, L0_attn_dt_proj_b, L0_attn_A_log,
                 L0_attn_D_param, L0_attn_out_proj_w, L0_ffn_w_gate, L0_ffn_w_up,
                 L0_ffn_w_down):
    x = np.asarray(L0_attn_x, dtype=np.float32)
    w_inT = np.ascontiguousarray(np.asarray(L0_attn_in_proj_w, np.float32).T)
    w_outT = np.ascontiguousarray(np.asarray(L0_attn_out_proj_w, np.float32).T)
    xp = np.asarray(L0_attn_x_proj_w, np.float32)
    w_xpT = np.zeros((E, 128), np.float32)
    w_xpT[:, 0:64] = xp[0:64].T
    w_xpT[:, 64:80] = xp[64:80].T
    w_xpT[:, 96:112] = xp[80:96].T
    w_dtT = np.ascontiguousarray(np.asarray(L0_attn_dt_proj_w, np.float32).T)
    w_up = np.asarray(L0_ffn_w_up, np.float32)
    w_gate = np.asarray(L0_ffn_w_gate, np.float32)
    w_down = np.asarray(L0_ffn_w_down, np.float32)
    ident = np.eye(128, dtype=np.float32)

    a_log = np.asarray(L0_attn_A_log, np.float32)
    conv_w = np.asarray(L0_attn_conv_w, np.float32)
    conv_b = np.asarray(L0_attn_conv_b, np.float32)
    dtb = np.asarray(L0_attn_dt_proj_b, np.float32)
    dpar = np.asarray(L0_attn_D_param, np.float32)

    in_maps = []
    for c in range(NCORE):
        b, q = c // 4, c % 4
        t0 = q * TCORE
        sl = x[b, max(0, t0 - HALO):t0 + TCORE, :]
        if q == 0:
            sl = np.concatenate([np.zeros((HALO, D), np.float32), sl], axis=0)
        xT = np.ascontiguousarray(sl.T)
        sm = np.zeros((128, 512), np.float32)
        for es in range(16):
            for n in range(NST):
                sm[:, OF_A + es * 16 + n] = a_log[es * 128:(es + 1) * 128, n]
        for ec in range(16):
            sm[:, OF_DTB + ec] = dtb[ec * 128:(ec + 1) * 128]
        for et in range(16):
            for k in range(4):
                sm[:, OF_CW + et * 4 + k] = conv_w[et * 128:(et + 1) * 128, 0, k]
            sm[:, OF_CB + et] = conv_b[et * 128:(et + 1) * 128]
            sm[:, OF_D + et] = dpar[et * 128:(et + 1) * 128]
        for j in range(3):
            u = 1.0 if j < q else 0.0
            sm[:, OF_U + j] = u
            sm[:, OF_V + j] = 1.0 - u
        sm[:, OF_ONE] = 1.0
        sm[0, OF_ONEROW:OF_ONEROW + 128] = 1.0
        in_maps.append({
            "xT": xT, "w_inT": w_inT, "w_outT": w_outT, "w_xpT": w_xpT, "w_dtT": w_dtT,
            "w_up": w_up, "w_gate": w_gate, "w_down": w_down, "smalls": sm, "ident": ident,
        })
    return in_maps


DEBUG = False


def _debug_outputs(em):
    nc, g = em.nc, em.g
    outs = {}
    for nm, t, shape in (("d_xln", "s_xln", [D, TT]), ("d_xres1", "s_xres1", [D, TT]),
                         ("d_xc", "s_xc", [E, TT]), ("d_sz", "s_sz", [E, TT]),
                         ("d_xa", "s_xa", [E, TCORE]), ("d_delta", "s_delta", [E, TCORE]),
                         ("d_yg", "s_yg", [E, TCORE]), ("d_xres2", "s_xres2", [D, TCORE]),
                         ("d_up", "s_up", [H, TT]), ("d_gate", "s_gate", [H, TT]),
                         ("d_gs", "s_gs", [H, TT])):
        outs[nm] = (nc.dram_tensor(nm, shape, F32, kind="ExternalOutput").ap(), g[t])
    sb = {"d_scales": em.scales, "d_P": em.p_sb, "d_S": em.carry, "d_hin": em.h_in,
          "d_sumd": em.sumdelta}
    sbo = {nm: nc.dram_tensor(nm, [128, t.shape[1]], F32, kind="ExternalOutput").ap()
           for nm, t in sb.items()}

    def _g(q):
        for nm, (o, s) in outs.items():
            em.inc("qG", q.dma_start(out=o[:, :], in_=s[:, :]), 16)
        for nm, o in sbo.items():
            em.inc("qG", q.dma_start(out=o[:, :], in_=sb[nm][:, :]), 16)
        em.w(q, "qG")
    em.on("gpsimd", _g)


_NC_CACHE = None


def kernel(**inputs):
    global _NC_CACHE
    in_maps = _host_inputs(**inputs)
    if _NC_CACHE is None:
        _NC_CACHE = Emit().build()
    res = run_bass_kernel_spmd(_NC_CACHE, in_maps, core_ids=list(range(NCORE)))
    out = np.empty((B_SZ, L_SEQ, D), dtype=np.float32)
    for c in range(NCORE):
        b, q = c // 4, c % 4
        out[b, q * TCORE:(q + 1) * TCORE, :] = np.asarray(res.results[c]["outT"]).T
    return out
